# version 02 (final)
# baseline (speedup 1.0000x reference)
import math
import numpy as np
import ml_dtypes
import concourse.bass as bass
import concourse.mybir as mybir
from concourse.bass_utils import run_bass_kernel_spmd

F32, BF16 = mybir.dt.float32, mybir.dt.bfloat16
AF = mybir.ActivationFunctionType
OP = mybir.AluOpType
AX = mybir.AxisListType.X
BFNP = ml_dtypes.bfloat16
NCORES = 8
TC = 2048
NTOK = 16384
DM = 1024

_RUN_COUNT = [0]
_BUILD_ONLY = [False]
_DBG = {}


class Prog:
    def __init__(self, name):
        self.name = name
        self.nc = bass.Bass("TRN2", target_bir_lowering=False)
        self.steps = []
        self.total = 0
        self.out_specs = {}

    def inp(self, name, shape, dtype=F32):
        return self.nc.dram_tensor(name, list(shape), dtype, kind="ExternalInput")

    def outp(self, name, shape, dtype=F32):
        self.out_specs[name] = (tuple(shape), np.float32 if dtype == F32 else BFNP)
        return self.nc.dram_tensor(name, list(shape), dtype, kind="ExternalOutput")

    def sb(self, name, shape, dtype=F32):
        return self.nc.alloc_sbuf_tensor(name, list(shape), dtype)

    def ps(self, name, shape, dtype=F32):
        return self.nc.alloc_psum_tensor(name, list(shape), dtype)

    def step(self, eng, fn, n=1, dma=False, concurrent=False):
        wait = self.steps[-1]["wait"] if (concurrent and self.steps) else self.total
        self.steps.append(dict(eng=eng, fn=fn, dma=dma, wait=wait, n=n))
        self.total += (16 * n) if dma else n

    def run(self, in_maps, trace=False):
        nc = self.nc
        final_total = self.total
        with nc.semaphore("gsem") as gsem, nc.Block() as block:
            def mk(eng_name):
                def body(eng):
                    for s in self.steps:
                        if s["eng"] != eng_name:
                            continue
                        eng.wait_ge(gsem, s["wait"])
                        insts = s["fn"](eng)
                        assert len(insts) == s["n"], (self.name, eng_name, len(insts), s["n"])
                        amt = 16 if s["dma"] else 1
                        for i in insts:
                            i.then_inc(gsem, amt)
                    if eng_name == "sync":
                        eng.wait_ge(gsem, final_total)
                return body
            block.tensor(mk("tensor"))
            block.vector(mk("vector"))
            block.scalar(mk("scalar"))
            block.gpsimd(mk("gpsimd"))
            block.sync(mk("sync"))
        if _BUILD_ONLY[0]:
            return [{k: np.zeros(s, d) for k, (s, d) in self.out_specs.items()}
                    for _ in range(NCORES)]
        res = run_bass_kernel_spmd(nc, in_maps, core_ids=list(range(NCORES)))
        _RUN_COUNT[0] += 1
        return res.results


def f32c(x):
    return np.ascontiguousarray(x, dtype=np.float32)


def bfc(x):
    assert x.dtype == BFNP, x.dtype
    return np.ascontiguousarray(x)



def ln_tile_steps(p, xin, part, width, m_col, r_col, sq, eps=1e-5):
    def red1(e, xin=xin, m=m_col, part=part):
        return [e.reduce_sum(m[:part, :], xin, axis=AX)]
    p.step("vector", red1)

    def sqr(e, xin=xin, sq=sq, part=part, width=width):
        return [e.activation(sq[:part, :width], xin, AF.Square)]
    p.step("scalar", sqr)

    def red2(e, sq=sq, r=r_col, part=part, width=width):
        return [e.reduce_sum(r[:part, :], sq[:part, :width], axis=AX)]
    p.step("vector", red2)

    inv = 1.0 / width

    def fin(e, m=m_col, r=r_col, part=part, inv=inv):
        i1 = e.tensor_scalar_mul(m[:part, :], m[:part, :], inv)
        i2 = e.tensor_scalar_mul(r[:part, :], r[:part, :], inv)
        return [i1, i2]
    p.step("vector", fin, n=2)

    def var(e, m=m_col, r=r_col, sq=sq, part=part):
        return [e.activation(sq[:part, 0:1], m[:part, :], AF.Square)]
    p.step("scalar", var)

    def sub(e, r=r_col, sq=sq, part=part):
        return [e.tensor_sub(r[:part, :], r[:part, :], sq[:part, 0:1])]
    p.step("vector", sub)

    def adde(e, r=r_col, part=part, eps=eps):
        return [e.tensor_scalar_add(r[:part, :], r[:part, :], float(eps))]
    p.step("vector", adde)

    def lnv(e, r=r_col, part=part):
        return [e.activation(r[:part, :], r[:part, :], AF.Ln)]
    p.step("scalar", lnv)

    def nhf(e, r=r_col, part=part):
        return [e.tensor_scalar_mul(r[:part, :], r[:part, :], -0.5)]
    p.step("vector", nhf)

    def expv(e, r=r_col, part=part):
        return [e.activation(r[:part, :], r[:part, :], AF.Exp)]
    p.step("scalar", expv)



def build_ln_launch(name, Trows, width, has_res, has_gb, double_ln,
                    out_f32, out_bf, out1_f32=False):
    p = Prog(name)
    ntiles = Trows // 128
    xin_d = p.inp("xin", [Trows, width])
    res_d = p.inp("res", [Trows, width]) if has_res else None
    if has_gb:
        g_d = p.inp("gt", [128, width])
        b_d = p.inp("bt", [128, width])
    o_f32_d = p.outp("o_f32", [Trows, width]) if out_f32 else None
    o_bf_d = p.outp("o_bf", [Trows, width], BF16) if out_bf else None
    o1_d = p.outp("o1_f32", [Trows, width]) if out1_f32 else None

    xin = p.sb("xin_s", [128, width])
    res = p.sb("res_s", [128, width]) if has_res else None
    sq = p.sb("sq_s", [128, width])
    cols = p.sb("cols_s", [128, 4])
    yout = p.sb("y_s", [128, width])
    ybf = p.sb("ybf_s", [128, width], BF16) if out_bf else None
    if has_gb:
        gt = p.sb("g_s", [128, width])
        bt = p.sb("b_s", [128, width])
        p.step("sync", lambda e: [e.dma_start(gt[:, :], g_d[:, :]),
                                  e.dma_start(bt[:, :], b_d[:, :])], n=2, dma=True)

    for t in range(ntiles):
        r0 = t * 128

        def ld(e, r0=r0):
            ins = [e.dma_start(xin[:, :], xin_d[r0:r0 + 128, :])]
            if has_res:
                ins.append(e.dma_start(res[:, :], res_d[r0:r0 + 128, :]))
            return ins
        p.step("sync", ld, n=2 if has_res else 1, dma=True)

        ln_tile_steps(p, xin[:, :], 128, width, cols[:, 0:1], cols[:, 1:2], sq)

        def norm(e):
            return [e.tensor_scalar(yout[:, :], xin[:, :], cols[:, 0:1],
                                    cols[:, 1:2], OP.subtract, OP.mult)]
        p.step("vector", norm)

        if has_gb:
            def gb1(e):
                return [e.tensor_mul(yout[:, :], yout[:, :], gt[:, :])]
            p.step("vector", gb1)

            def gb2(e):
                return [e.tensor_add(yout[:, :], yout[:, :], bt[:, :])]
            p.step("vector", gb2)

        if double_ln:
            def addr(e):
                return [e.tensor_add(yout[:, :], yout[:, :], res[:, :])]
            p.step("vector", addr)
            if out1_f32:
                def st1(e, r0=r0):
                    return [e.dma_start(o1_d[r0:r0 + 128, :], yout[:, :])]
                p.step("sync", st1, dma=True)
            if not (out_f32 or out_bf):
                continue
            ln_tile_steps(p, yout[:, :], 128, width, cols[:, 2:3], cols[:, 3:4], sq)

            def norm2(e):
                return [e.tensor_scalar(xin[:, :], yout[:, :], cols[:, 2:3],
                                        cols[:, 3:4], OP.subtract, OP.mult)]
            p.step("vector", norm2)
            final = xin
        elif has_res:
            def addr2(e):
                return [e.tensor_add(yout[:, :], yout[:, :], res[:, :])]
            p.step("vector", addr2)
            final = yout
        else:
            final = yout

        if out_bf:
            def cbf(e, final=final):
                return [e.tensor_copy(ybf[:, :], final[:, :])]
            p.step("vector", cbf)

        def st(e, r0=r0, final=final):
            ins = []
            if out_f32:
                ins.append(e.dma_start(o_f32_d[r0:r0 + 128, :], final[:, :]))
            if out_bf:
                ins.append(e.dma_start(o_bf_d[r0:r0 + 128, :], ybf[:, :]))
            return ins
        p.step("sync", st, n=int(out_f32) + int(out_bf), dma=True)
    return p



def build_mm_launch(name, K, M, N, bias, act, residual, out_f32, out_bf,
                    b_is_f32=False):
    p = Prog(name)
    a_d = p.inp("A", [K, M])
    b_d = p.inp("B", [K, N], F32 if b_is_f32 else BF16)
    bias_d = p.inp("bias", [1, M]) if bias else None
    res_d = p.inp("R", [M, N]) if residual else None
    one_d = p.inp("one1", [1, 1])
    of_d = p.outp("o_f32", [M, N]) if out_f32 else None
    ob_d = p.outp("o_bf", [M, N], BF16) if out_bf else None

    kcs = [min(128, K - i * 128) for i in range((K + 127) // 128)]
    nK = len(kcs)
    mcs = [min(128, M - i * 128) for i in range((M + 127) // 128)]
    nM = len(mcs)
    nN = N // 512
    MBLK = min(M, 2048)
    mb_list = []
    m0 = 0
    while m0 < M:
        mb = min(MBLK, M - m0)
        mb_list.append((m0, mb))
        m0 += mb
    ngroups = (nK + 7) // 8
    multi = ngroups > 1
    assert not (multi and len(mb_list) > 1)

    af32 = p.sb("af32", [128, MBLK])
    abf = [p.sb(f"abf{i}", [128, MBLK], BF16) for i in range(8)]
    bbf = [p.sb(f"bbf{i}", [128, N], BF16) for i in range(8 if multi else nK)]
    bf32 = p.sb("bf32", [128, N]) if b_is_f32 else None
    acc = [p.sb(f"acc{i}", [128, N]) for i in range(nM)] if multi else None
    onesb = p.sb("onesb", [1, 1])
    biasr = p.sb("biasr", [1, M]) if bias else None
    bcol = p.sb("bcol", [128, nM]) if bias else None
    rtile = p.sb("rtile", [128, 512]) if residual else None
    otile = p.sb("otile", [128, 512])
    obf_t = p.sb("obf_t", [128, 512], BF16) if out_bf else None
    psum = [p.ps(f"psA{i}", [128, 512]) for i in range(2)]
    psB = p.ps("psB", [128, 1])

    p.step("sync", lambda e: [e.dma_start(onesb[:, :], one_d[:, :])], dma=True)
    if bias:
        p.step("sync", lambda e: [e.dma_start(biasr[:, :], bias_d[:, :])], dma=True)
        for m in range(nM):
            def bmm(e, m=m):
                return [e.matmul(psB[:mcs[m], :],
                                 biasr[0:1, m * 128:m * 128 + mcs[m]],
                                 onesb[0:1, :], start=True, stop=True)]
            p.step("tensor", bmm)

            def bcp(e, m=m):
                return [e.tensor_copy(bcol[:mcs[m], m:m + 1], psB[:mcs[m], :])]
            p.step("vector", bcp)

    if not multi:
        for k in range(nK):
            if b_is_f32:
                def ldb(e, k=k):
                    return [e.dma_start(bf32[:kcs[k], :], b_d[k * 128:k * 128 + kcs[k], :])]
                p.step("sync", ldb, dma=True)

                def cvb(e, k=k):
                    return [e.tensor_copy(bbf[k][:kcs[k], :], bf32[:kcs[k], :])]
                p.step("vector", cvb)
            else:
                def ldb2(e, k=k):
                    return [e.dma_start(bbf[k][:kcs[k], :], b_d[k * 128:k * 128 + kcs[k], :])]
                p.step("sync", ldb2, dma=True)

    pscnt = [0]

    def emit_epilogue(m, n, src):
        mc = mcs[m]

        def epi(e, m=m, n=n, src=src, mc=mc):
            if bias:
                return [e.tensor_scalar_add(otile[:mc, :], src, bcol[:mc, m:m + 1])]
            return [e.tensor_copy(otile[:mc, :], src)]
        p.step("vector", epi)
        if act == "lrelu":
            def lr(e, mc=mc):
                return [e.scalar_tensor_tensor(otile[:mc, :], otile[:mc, :], 0.01,
                                               otile[:mc, :], OP.mult, OP.max)]
            p.step("vector", lr)
        if residual:
            def ldr(e, m=m, n=n, mc=mc):
                return [e.dma_start(rtile[:mc, :],
                                    res_d[m * 128:m * 128 + mc, n * 512:(n + 1) * 512])]
            p.step("sync", ldr, dma=True)

            def ar(e, mc=mc):
                return [e.tensor_add(otile[:mc, :], otile[:mc, :], rtile[:mc, :])]
            p.step("vector", ar)
        if out_bf:
            def cb(e, mc=mc):
                return [e.tensor_copy(obf_t[:mc, :], otile[:mc, :])]
            p.step("vector", cb)

        def st(e, m=m, n=n, mc=mc):
            ins = []
            if out_f32:
                ins.append(e.dma_start(of_d[m * 128:m * 128 + mc, n * 512:(n + 1) * 512],
                                       otile[:mc, :]))
            if out_bf:
                ins.append(e.dma_start(ob_d[m * 128:m * 128 + mc, n * 512:(n + 1) * 512],
                                       obf_t[:mc, :]))
            return ins
        p.step("sync", st, n=int(out_f32) + int(out_bf), dma=True)

    for (m0, mbw) in mb_list:
        mcl = m0 // 128
        mch = (m0 + mbw + 127) // 128
        for g in range(ngroups):
            k0 = g * 8
            k1 = min(nK, k0 + 8)
            for k in range(k0, k1):
                def lda(e, k=k, m0=m0, mbw=mbw):
                    return [e.dma_start(af32[:kcs[k], :mbw],
                                        a_d[k * 128:k * 128 + kcs[k], m0:m0 + mbw])]
                p.step("sync", lda, dma=True)

                def cva(e, k=k, k0=k0, mbw=mbw):
                    return [e.tensor_copy(abf[k - k0][:kcs[k], :mbw], af32[:kcs[k], :mbw])]
                p.step("vector", cva)
                if multi:
                    if b_is_f32:
                        def ldbg(e, k=k):
                            return [e.dma_start(bf32[:kcs[k], :],
                                                b_d[k * 128:k * 128 + kcs[k], :])]
                        p.step("sync", ldbg, dma=True)

                        def cvbg(e, k=k, k0=k0):
                            return [e.tensor_copy(bbf[k - k0][:kcs[k], :], bf32[:kcs[k], :])]
                        p.step("vector", cvbg)
                    else:
                        def ldbg2(e, k=k, k0=k0):
                            return [e.dma_start(bbf[k - k0][:kcs[k], :],
                                                b_d[k * 128:k * 128 + kcs[k], :])]
                        p.step("sync", ldbg2, dma=True)
            for m in range(mcl, mch):
                mc = mcs[m]
                am0 = m * 128 - m0
                for n in range(nN):
                    pb = psum[pscnt[0] % 2]
                    pscnt[0] += 1

                    def mms(e, m=m, n=n, k0=k0, k1=k1, pb=pb, mc=mc, am0=am0):
                        ins = []
                        for k in range(k0, k1):
                            ins.append(e.matmul(
                                pb[:mc, :],
                                abf[k - k0][:kcs[k], am0:am0 + mc],
                                bbf[k - k0][:kcs[k], n * 512:(n + 1) * 512],
                                start=(k == k0), stop=(k == k1 - 1)))
                        return [ins[-1]]
                    p.step("tensor", mms)
                    if multi:
                        def accum(e, m=m, n=n, pb=pb, g=g, mc=mc):
                            if g == 0:
                                return [e.tensor_copy(acc[m][:mc, n * 512:(n + 1) * 512],
                                                      pb[:mc, :])]
                            return [e.tensor_add(acc[m][:mc, n * 512:(n + 1) * 512],
                                                 acc[m][:mc, n * 512:(n + 1) * 512],
                                                 pb[:mc, :])]
                        p.step("vector", accum)
                    else:
                        emit_epilogue(m, n, pb[:mc, :])
    if multi:
        for m in range(nM):
            for n in range(nN):
                emit_epilogue(m, n, acc[m][:mcs[m], n * 512:(n + 1) * 512])
    return p



def build_attn_launch(name, npairs, T, DV, causal, q_prescale=1.0):
    p = Prog(name)
    q_d = p.inp("Qd", [npairs, 64, T], BF16)
    k_d = p.inp("Kd", [npairs, 64, T], BF16)
    v_d = p.inp("Vt", [npairs, T, DV], BF16)
    ones64_d = p.inp("ones64", [64, 1])
    ones1r_d = p.inp("ones1r", [1, 128])
    one1_d = p.inp("one1", [1, 1])
    ones128b_d = p.inp("ones128b", [128, 1], BF16)
    if causal:
        md_d = p.inp("Mdiag", [4, 128, 512], BF16)
    o_d = p.outp("O", [npairs, T, DV], BF16)

    nJT = T // 128
    nIC = T // 512
    nDV = DV // 128 if DV >= 128 else 1
    dvw = min(DV, 512)

    qsb = p.sb("qsb", [64, T], BF16)
    ksb = p.sb("ksb", [64, T], BF16)
    vsb = p.sb("vsb", [128, nJT * DV], BF16)
    esb = p.sb("esb", [128, nJT * 512], BF16)
    sqch = p.sb("sqch", [64, 512])
    nmaxes = p.sb("nmaxes", [1, 2 * (T // 512)])
    zrow = p.sb("zrow", [1, 512])
    rz = p.sb("rz", [128, 4])
    csc = p.sb("csc", [1, 4])
    bias_c = p.sb("bias_c", [128, 1])
    ones64 = p.sb("ones64s", [64, 1])
    ones1r = p.sb("ones1rs", [1, 128])
    one1 = p.sb("one1s", [1, 1])
    ones128b = p.sb("o128bs", [128, 1], BF16)
    if causal:
        mdiag = p.sb("mdiag", [128, 4 * 512], BF16)
    otile = [p.sb(f"ot{i}", [128, dvw], BF16) for i in range(2)]

    psS = p.ps("psS", [128, 1024])
    psAV = [p.ps(f"psAV{i}", [128, dvw]) for i in range(4)]
    psZ = p.ps("psZ", [1, 512])
    psZc = p.ps("psZc", [128, 4])

    def ldc(e):
        ins = [e.dma_start(ones64[:, :], ones64_d[:, :]),
               e.dma_start(ones1r[:, :], ones1r_d[:, :]),
               e.dma_start(one1[:, :], one1_d[:, :]),
               e.dma_start(ones128b[:, :], ones128b_d[:, :])]
        if causal:
            for d in range(4):
                ins.append(e.dma_start(mdiag[:, d * 512:(d + 1) * 512],
                                       md_d[d, :, :]))
        return ins
    p.step("sync", ldc, n=4 + (4 if causal else 0), dma=True)

    for pr in range(npairs):
        def ldqk(e, pr=pr):
            return [e.dma_start(qsb[:, :], q_d[pr, :, :]),
                    e.dma_start(ksb[:, :], k_d[pr, :, :])]
        p.step("sync", ldqk, n=2, dma=True)

        if q_prescale != 1.0:
            def qpres(e):
                return [e.tensor_scalar_mul(qsb[:, :], qsb[:, :], float(q_prescale))]
            p.step("vector", qpres)

        def ldv(e, pr=pr):
            ins = []
            for jt in range(nJT):
                ins.append(e.dma_start(vsb[:, jt * DV:(jt + 1) * DV],
                                       v_d[pr, jt * 128:(jt + 1) * 128, :]))
            return ins
        p.step("sync", ldv, n=nJT, dma=True)

        for (row, src) in ((0, qsb), (1, ksb)):
            for ch in range(T // 512):
                def sqs(e, src=src, ch=ch):
                    return [e.activation(sqch[:, :], src[:, ch * 512:(ch + 1) * 512],
                                         AF.Square)]
                p.step("scalar", sqs)

                def nmm(e, ch=ch):
                    return [e.matmul(psZ[0:1, :], ones64[:, :], sqch[:, :],
                                     start=True, stop=True)]
                p.step("tensor", nmm)

                def ncp(e, row=row, ch=ch):
                    nch = T // 512
                    return [e.reduce_max(nmaxes[0:1, row * nch + ch:row * nch + ch + 1],
                                         psZ[0:1, :], axis=AX)]
                p.step("vector", ncp)

        nch = T // 512

        def rmx(e, nch=nch):
            i1 = e.reduce_max(csc[0:1, 0:1], nmaxes[0:1, 0:nch], axis=AX)
            i2 = e.reduce_max(csc[0:1, 1:2], nmaxes[0:1, nch:2 * nch], axis=AX)
            return [i1, i2]
        p.step("vector", rmx, n=2)

        def csqrt(e):
            return [e.activation(csc[0:1, 0:2], csc[0:1, 0:2], AF.Sqrt)]
        p.step("scalar", csqrt)

        def cmul(e):
            return [e.tensor_mul(csc[0:1, 2:3], csc[0:1, 0:1], csc[0:1, 1:2])]
        p.step("vector", cmul)

        def cneg(e):
            return [e.tensor_scalar_mul(csc[0:1, 2:3], csc[0:1, 2:3], -1.02)]
        p.step("vector", cneg)

        def cbc(e):
            return [e.matmul(psZc[:, 0:1], ones1r[0:1, :], csc[0:1, 2:3],
                             start=True, stop=True)]
        p.step("tensor", cbc)

        def cbcp(e):
            return [e.tensor_copy(bias_c[:, :], psZc[:, 0:1])]
        p.step("vector", cbcp)

        for ic in range(nIC):
            njt = min(nJT, 4 * (ic + 1)) if causal else nJT
            for j2 in range(0, njt, 2):
                def smm(e, ic=ic, j2=j2, njt=njt):
                    ins = []
                    for jt in range(j2, min(njt, j2 + 2)):
                        ins.append(e.matmul(psS[:, (jt % 2) * 512:(jt % 2) * 512 + 512],
                                            ksb[:, jt * 128:(jt + 1) * 128],
                                            qsb[:, ic * 512:(ic + 1) * 512],
                                            start=True, stop=True))
                    return [ins[-1]]
                p.step("tensor", smm)

                def sexp(e, ic=ic, j2=j2, njt=njt):
                    w = (min(njt, j2 + 2) - j2) * 512
                    return [e.activation(esb[:, j2 * 512:j2 * 512 + w],
                                         psS[:, 0:w], AF.Exp, bias=bias_c[:, 0:1])]
                p.step("scalar", sexp)
            if causal:
                for d in range(4):
                    jt = 4 * ic + d
                    if jt >= njt:
                        continue

                    def msk(e, jt=jt, d=d):
                        return [e.tensor_mul(esb[:, jt * 512:(jt + 1) * 512],
                                             esb[:, jt * 512:(jt + 1) * 512],
                                             mdiag[:, d * 512:(d + 1) * 512])]
                    p.step("vector", msk)

            def zmm(e, njt=njt):
                ins = []
                for jt in range(njt):
                    ins.append(e.matmul(psZ[0:1, :], ones128b[:, :],
                                        esb[:, jt * 512:(jt + 1) * 512],
                                        start=(jt == 0), stop=(jt == njt - 1)))
                return [ins[-1]]
            p.step("tensor", zmm)

            def zcp(e):
                return [e.tensor_copy(zrow[0:1, :], psZ[0:1, :])]
            p.step("vector", zcp)

            def ztr(e):
                ins = []
                for s in range(4):
                    ins.append(e.matmul(psZc[:, s:s + 1],
                                        zrow[0:1, s * 128:(s + 1) * 128],
                                        one1[0:1, :], start=True, stop=True))
                return [ins[-1]]
            p.step("tensor", ztr)

            def zrec(e):
                return [e.reciprocal(rz[:, :], psZc[:, :])]
            p.step("vector", zrec)

            for s in range(4):
                def avm(e, ic=ic, s=s, njt=njt):
                    ins = []
                    for jt in range(njt):
                        ins.append(e.matmul(psAV[s][:, :],
                                            esb[:, jt * 512 + s * 128:jt * 512 + s * 128 + 128],
                                            vsb[:, jt * DV:jt * DV + dvw],
                                            start=(jt == 0), stop=(jt == njt - 1)))
                    return [ins[-1]]
                p.step("tensor", avm)

                def avev(e, s=s):
                    return [e.tensor_scalar(otile[s % 2][:, :], psAV[s][:, :],
                                            rz[:, s:s + 1], None, OP.mult)]
                p.step("vector", avev)

                def avst(e, pr=pr, ic=ic, s=s):
                    r0 = ic * 512 + s * 128
                    return [e.dma_start(o_d[pr, r0:r0 + 128, :], otile[s % 2][:, :])]
                p.step("sync", avst, dma=True)
    return p



def build_latt_launch(name):
    p = Prog(name)
    q_d = p.inp("Qd", [512, TC], BF16)
    k_d = p.inp("Kd", [512, TC + 512], BF16)
    v_d = p.inp("Vt", [TC + 512, 512], BF16)
    m_d = p.inp("maskadd", [4, 512, 1024])
    eye_d = p.inp("eyeb", [128, 128], BF16)
    o_d = p.outp("oT", [512, TC], BF16)

    qsb = [p.sb(f"q{i}", [128, TC], BF16) for i in range(4)]
    ksb = [p.sb(f"k{i}", [128, TC + 512], BF16) for i in range(4)]
    vsb = p.sb("vsb", [128, 20 * 512], BF16)
    msb = p.sb("msb", [128, 4 * 4 * 1024])
    eye = p.sb("eye", [128, 128], BF16)
    ssb = p.sb("ssb", [128, 1024])
    ebf = p.sb("ebf", [128, 1024], BF16)
    etall = p.sb("etall", [128, 8 * 512], BF16)
    cols = p.sb("colsA", [128, 2])
    osb = [p.sb(f"osb{i}", [128, TC], BF16) for i in range(4)]
    psS = p.ps("psS2", [128, 1024])
    psT = [p.ps(f"psT{i}", [128, 128], BF16) for i in range(2)]
    psO = p.ps("psO", [64, 512])

    def ldc(e):
        ins = [e.dma_start(eye[:, :], eye_d[:, :])]
        for i in range(4):
            ins.append(e.dma_start(qsb[i][:, :], q_d[i * 128:(i + 1) * 128, :]))
            ins.append(e.dma_start(ksb[i][:, :], k_d[i * 128:(i + 1) * 128, :]))
        for jt in range(20):
            ins.append(e.dma_start(vsb[:, jt * 512:(jt + 1) * 512],
                                   v_d[jt * 128:(jt + 1) * 128, :]))
        for w in range(4):
            for it in range(4):
                ins.append(e.dma_start(
                    msb[:, (w * 4 + it) * 1024:(w * 4 + it + 1) * 1024],
                    m_d[w, it * 128:(it + 1) * 128, :]))
        return ins
    p.step("sync", ldc, n=1 + 8 + 20 + 16, dma=True)

    for w in range(4):
        for h in range(8):
            qt, qo = qsb[h // 2], (h % 2) * 64
            kt = ksb[h // 2]
            for it in range(4):
                def smm(e, w=w, h=h, it=it, qt=qt, qo=qo, kt=kt):
                    ins = []
                    for jh in range(2):
                        ins.append(e.matmul(
                            psS[:, jh * 512:(jh + 1) * 512],
                            qt[qo:qo + 64, w * 512 + it * 128:w * 512 + it * 128 + 128],
                            kt[qo:qo + 64, w * 512 + jh * 512:w * 512 + (jh + 1) * 512],
                            start=True, stop=True))
                    return [ins[-1]]
                p.step("tensor", smm)

                def madd(e, w=w, it=it):
                    return [e.tensor_add(ssb[:, :], psS[:, :],
                                         msb[:, (w * 4 + it) * 1024:(w * 4 + it + 1) * 1024])]
                p.step("vector", madd)

                def rmx(e):
                    return [e.reduce_max(cols[:, 0:1], ssb[:, :], axis=AX)]
                p.step("vector", rmx)

                def rng(e):
                    return [e.tensor_scalar_mul(cols[:, 0:1], cols[:, 0:1], -1.0)]
                p.step("vector", rng)

                def sexp(e):
                    return [e.activation(ebf[:, :], ssb[:, :], AF.Exp,
                                         bias=cols[:, 0:1])]
                p.step("scalar", sexp)

                def zs(e):
                    return [e.reduce_sum(cols[:, 1:2], ebf[:, :], axis=AX)]
                p.step("vector", zs)

                def zrec(e):
                    return [e.reciprocal(cols[:, 1:2], cols[:, 1:2])]
                p.step("vector", zrec)

                def escale(e):
                    return [e.tensor_scalar(ebf[:, :], ebf[:, :], cols[:, 1:2],
                                            None, OP.mult)]
                p.step("vector", escale)

                for jt in range(8):
                    def etr(e, jt=jt):
                        return [e.transpose(psT[jt % 2][:, :],
                                            ebf[:, jt * 128:(jt + 1) * 128], eye[:, :])]
                    p.step("tensor", etr)

                    def etc(e, jt=jt, it=it):
                        return [e.tensor_copy(etall[:, jt * 512 + it * 128:jt * 512 + it * 128 + 128],
                                              psT[jt % 2][:, :])]
                    p.step("vector", etc)
            def avm(e, w=w, h=h):
                ins = []
                for jt in range(8):
                    ins.append(e.matmul(psO[:, :],
                                        vsb[:, (w * 4 + jt) * 512 + h * 64:(w * 4 + jt) * 512 + h * 64 + 64],
                                        etall[:, jt * 512:(jt + 1) * 512],
                                        start=(jt == 0), stop=(jt == 7)))
                return [ins[-1]]
            p.step("tensor", avm)

            def avc(e, w=w, h=h):
                return [e.tensor_copy(osb[h // 2][(h % 2) * 64:(h % 2) * 64 + 64,
                                                  w * 512:(w + 1) * 512], psO[:, :])]
            p.step("vector", avc)

    def sto(e):
        return [e.dma_start(o_d[i * 128:(i + 1) * 128, :], osb[i][:, :])
                for i in range(4)]
    p.step("sync", sto, n=4, dma=True)
    return p



def build_rope2_launch(name):
    p = Prog(name)
    q_d = p.inp("q", [TC, 512], BF16)
    k_d = p.inp("k", [TC, 512], BF16)
    qc_d = p.inp("qc", [TC, 512])
    qs_d = p.inp("qs", [TC, 512])
    kc_d = p.inp("kc", [TC, 512])
    ks_d = p.inp("ks", [TC, 512])
    qsc_d = p.inp("qscale", [128, 512])
    ksc_d = p.inp("kscale", [128, 512])
    qo_d = p.outp("qo", [TC, 512], BF16)
    ko_d = p.outp("ko", [TC, 512], BF16)

    xin = p.sb("xin", [128, 512], BF16)
    tabc = p.sb("tabc", [128, 512])
    tabs = p.sb("tabs", [128, 512])
    scb = p.sb("scb", [128, 512])
    scb2 = p.sb("scb2", [128, 512])
    xn = p.sb("xn", [128, 512])
    xp = p.sb("xp", [128, 512])
    tmp1 = p.sb("tmp1", [128, 512])
    tmp2 = p.sb("tmp2", [128, 512])
    sq = p.sb("sqr", [128, 512])
    ss = p.sb("ss", [128, 8])
    xob = p.sb("xob", [128, 512], BF16)

    def ldsc(e):
        return [e.dma_start(scb[:, :], qsc_d[:, :]),
                e.dma_start(scb2[:, :], ksc_d[:, :])]
    p.step("sync", ldsc, n=2, dma=True)

    ntiles = TC // 128
    for t in range(ntiles):
        r0 = t * 128
        for (src_d, c_d, s_d, sc, out_d, fin_scale) in (
                (q_d, qc_d, qs_d, scb, qo_d, 8.0),
                (k_d, kc_d, ks_d, scb2, ko_d, 1.0)):
            def ld(e, src_d=src_d, c_d=c_d, s_d=s_d, r0=r0):
                return [e.dma_start(xin[:, :], src_d[r0:r0 + 128, :]),
                        e.dma_start(tabc[:, :], c_d[r0:r0 + 128, :]),
                        e.dma_start(tabs[:, :], s_d[r0:r0 + 128, :])]
            p.step("sync", ld, n=3, dma=True)

            def sqs(e):
                return [e.activation(sq[:, :], xin[:, :], AF.Square)]
            p.step("scalar", sqs)

            def red(e):
                ins = []
                for h in range(8):
                    ins.append(e.reduce_sum(ss[:, h:h + 1],
                                            sq[:, h * 64:(h + 1) * 64], axis=AX))
                return ins
            p.step("vector", red, n=8)

            def clmp(e):
                return [e.tensor_scalar_max(ss[:, :], ss[:, :], 1e-24)]
            p.step("vector", clmp)

            def rln(e):
                return [e.activation(ss[:, :], ss[:, :], AF.Ln)]
            p.step("scalar", rln)

            def rng(e):
                return [e.tensor_scalar_mul(ss[:, :], ss[:, :], -0.5)]
            p.step("vector", rng)

            def rexp(e):
                return [e.activation(ss[:, :], ss[:, :], AF.Exp)]
            p.step("scalar", rexp)

            def nrm(e):
                ins = []
                for h in range(8):
                    ins.append(e.tensor_scalar(xn[:, h * 64:(h + 1) * 64],
                                               xin[:, h * 64:(h + 1) * 64],
                                               ss[:, h:h + 1], None, OP.mult))
                return ins
            p.step("vector", nrm, n=8)

            def nsc(e, sc=sc):
                return [e.tensor_mul(xn[:, :], xn[:, :], sc[:, :])]
            p.step("vector", nsc)

            def rotp(e):
                ins = []
                for h in range(8):
                    lo, hi = h * 64, h * 64 + 32
                    ins.append(e.tensor_mul(tmp1[:, lo:lo + 32], xn[:, lo:lo + 32],
                                            tabc[:, lo:lo + 32]))
                    ins.append(e.tensor_mul(tmp2[:, lo:lo + 32], xn[:, hi:hi + 32],
                                            tabs[:, lo:lo + 32]))
                    ins.append(e.tensor_mul(tmp1[:, hi:hi + 32], xn[:, hi:hi + 32],
                                            tabc[:, hi:hi + 32]))
                    ins.append(e.tensor_mul(tmp2[:, hi:hi + 32], xn[:, lo:lo + 32],
                                            tabs[:, hi:hi + 32]))
                return ins
            p.step("vector", rotp, n=32)

            def rotc(e):
                ins = []
                for h in range(8):
                    lo, hi = h * 64, h * 64 + 32
                    ins.append(e.tensor_sub(xp[:, lo:lo + 32], tmp1[:, lo:lo + 32],
                                            tmp2[:, lo:lo + 32]))
                    ins.append(e.tensor_add(xp[:, hi:hi + 32], tmp1[:, hi:hi + 32],
                                            tmp2[:, hi:hi + 32]))
                return ins
            p.step("vector", rotc, n=16)

            def fin(e, fin_scale=fin_scale):
                return [e.tensor_scalar(xob[:, :], xp[:, :], fin_scale, None, OP.mult)]
            p.step("vector", fin)

            def st(e, out_d=out_d, r0=r0):
                return [e.dma_start(out_d[r0:r0 + 128, :], xob[:, :])]
            p.step("sync", st, dma=True)
    return p



def build_rope3_launch(name):
    p = Prog(name)
    q_d = p.inp("q", [TC, 1024], BF16)
    k_d = p.inp("k", [TC, 1024], BF16)
    qc_d = p.inp("qc", [TC, 512])
    qs_d = p.inp("qs", [TC, 512])
    kc_d = p.inp("kc", [TC, 512])
    ks_d = p.inp("ks", [TC, 512])
    qo_d = p.outp("qo", [TC, 1024], BF16)
    ko_d = p.outp("ko", [TC, 1024], BF16)

    xin = p.sb("xin", [128, 1024], BF16)
    tabc = p.sb("tabc", [128, 512])
    tabs = p.sb("tabs", [128, 512])
    tmp1 = p.sb("tmp1", [128, 512])
    tmp2 = p.sb("tmp2", [128, 512])
    tmp3 = p.sb("tmp3", [128, 512])
    tmp4 = p.sb("tmp4", [128, 512])
    xob = p.sb("xob", [128, 1024], BF16)

    for t in range(TC // 128):
        r0 = t * 128
        for (src_d, c_d, s_d, out_d) in ((q_d, qc_d, qs_d, qo_d),
                                         (k_d, kc_d, ks_d, ko_d)):
            def ld(e, src_d=src_d, c_d=c_d, s_d=s_d, r0=r0):
                return [e.dma_start(xin[:, :], src_d[r0:r0 + 128, :]),
                        e.dma_start(tabc[:, :], c_d[r0:r0 + 128, :]),
                        e.dma_start(tabs[:, :], s_d[r0:r0 + 128, :])]
            p.step("sync", ld, n=3, dma=True)

            def rotp(e):
                ins = []
                for h in range(16):
                    lo, hi = h * 64, h * 64 + 32
                    tl = h * 32
                    ins.append(e.tensor_mul(tmp1[:, tl:tl + 32], xin[:, lo:lo + 32],
                                            tabc[:, tl:tl + 32]))
                    ins.append(e.tensor_mul(tmp2[:, tl:tl + 32], xin[:, hi:hi + 32],
                                            tabs[:, tl:tl + 32]))
                    ins.append(e.tensor_mul(tmp3[:, tl:tl + 32], xin[:, hi:hi + 32],
                                            tabc[:, tl:tl + 32]))
                    ins.append(e.tensor_mul(tmp4[:, tl:tl + 32], xin[:, lo:lo + 32],
                                            tabs[:, tl:tl + 32]))
                return ins
            p.step("vector", rotp, n=64)

            def rotc(e):
                ins = []
                for h in range(16):
                    lo, hi = h * 64, h * 64 + 32
                    tl = h * 32
                    ins.append(e.tensor_sub(xob[:, lo:lo + 32], tmp1[:, tl:tl + 32],
                                            tmp2[:, tl:tl + 32]))
                    ins.append(e.tensor_add(xob[:, hi:hi + 32], tmp3[:, tl:tl + 32],
                                            tmp4[:, tl:tl + 32]))
                return ins
            p.step("vector", rotc, n=32)

            def st(e, out_d=out_d, r0=r0):
                return [e.dma_start(out_d[r0:r0 + 128, :], xob[:, :])]
            p.step("sync", st, dma=True)
    return p



def build_geglu_launch(name):
    p = Prog(name)
    HID = 2730
    a_d = p.inp("a", [HID, TC])
    g_d = p.inp("g", [HID, TC])
    o_d = p.outp("o_bf", [HID, TC], BF16)
    at = p.sb("at", [128, TC])
    gt = p.sb("gt", [128, TC])
    ob = p.sb("ob", [128, TC], BF16)
    chunks = [(i * 128, min(128, HID - i * 128)) for i in range((HID + 127) // 128)]
    for (c0, cw) in chunks:
        def ld(e, c0=c0, cw=cw):
            return [e.dma_start(at[:cw, :], a_d[c0:c0 + cw, :]),
                    e.dma_start(gt[:cw, :], g_d[c0:c0 + cw, :])]
        p.step("sync", ld, n=2, dma=True)

        def gel(e, cw=cw):
            return [e.activation(gt[:cw, :], gt[:cw, :], AF.Gelu)]
        p.step("scalar", gel)

        def mul(e, cw=cw):
            return [e.tensor_mul(ob[:cw, :], at[:cw, :], gt[:cw, :])]
        p.step("vector", mul)

        def st(e, c0=c0, cw=cw):
            return [e.dma_start(o_d[c0:c0 + cw, :], ob[:cw, :])]
        p.step("sync", st, dma=True)
    return p



def build_lora_launch(name, debug=False):
    p = Prog(name)
    d_dd = p.inp("dT", [1024, TC], BF16)
    res_d = p.inp("resT", [1024, TC])
    rw_d = p.inp("rwT", [1024, 8])
    rb_d = p.inp("rb", [128, 8])
    la_d = p.inp("loraA", [1024, 8])
    lb_d = p.inp("loraB2d", [64, 1024])
    eye_d = p.inp("eyeb", [128, 128], BF16)
    o_d = p.outp("o_f32", [1024, TC])

    dsb = [p.sb(f"d{i}", [128, TC], BF16) for i in range(8)]
    rwf = p.sb("rwf", [128, 8])
    rwb = p.sb("rwb", [128, 8], BF16)
    laf = p.sb("laf", [128, 8])
    lab = p.sb("lab", [128, 8], BF16)
    lbf = p.sb("lbf", [64, 1024])
    lbb = p.sb("lbb", [64, 1024], BF16)
    rbt = p.sb("rbt", [128, 8])
    eye = p.sb("eye2", [128, 128], BF16)
    lg = p.sb("lg", [128, 8])
    gat_b = p.sb("gat_b", [128, 8], BF16)
    gat_f = p.sb("gat_f", [128, 8])
    mid = p.sb("mid", [128, 8])
    mp = p.sb("mp", [128, 64])
    mpb = p.sb("mpb", [128, 64], BF16)
    mpt = p.sb("mpt", [64, TC], BF16)
    cols = p.sb("colsL", [128, 2])
    rtile = p.sb("rtileL", [128, 512])
    otile = p.sb("otileL", [128, 512])
    psL = p.ps("psL", [128, 8])
    psM = p.ps("psM", [128, 8])
    psT = p.ps("psTL", [128, 128], BF16)
    psO = p.ps("psOL", [128, 512])

    def ldc(e):
        ins = [e.dma_start(eye[:, :], eye_d[:, :]),
               e.dma_start(rwf[:, :], rw_d[0:128, :]),
               e.dma_start(laf[:, :], la_d[0:128, :]),
               e.dma_start(lbf[:, :], lb_d[:, :]),
               e.dma_start(rbt[:, :], rb_d[:, :])]
        for i in range(8):
            ins.append(e.dma_start(dsb[i][:, :], d_dd[i * 128:(i + 1) * 128, :]))
        return ins
    p.step("sync", ldc, n=13, dma=True)
    rw_ch = p.sb("rw_ch", [128, 8 * 8], BF16)
    la_ch = p.sb("la_ch", [128, 8 * 8], BF16)
    for k in range(8):
        def ldk(e, k=k):
            return [e.dma_start(rwf[:, :], rw_d[k * 128:(k + 1) * 128, :]),
                    e.dma_start(laf[:, :], la_d[k * 128:(k + 1) * 128, :])]
        p.step("sync", ldk, n=2, dma=True)

        def cvk(e, k=k):
            return [e.tensor_copy(rw_ch[:, k * 8:(k + 1) * 8], rwf[:, :]),
                    e.tensor_copy(la_ch[:, k * 8:(k + 1) * 8], laf[:, :])]
        p.step("vector", cvk, n=2)

    def cvlb(e):
        return [e.tensor_copy(lbb[:, :], lbf[:, :])]
    p.step("vector", cvlb)

    for t in range(TC // 128):
        c0 = t * 128

        def lmm(e, c0=c0):
            ins = []
            for k in range(8):
                ins.append(e.matmul(psL[:, :], dsb[k][:, c0:c0 + 128],
                                    rw_ch[:, k * 8:(k + 1) * 8],
                                    start=(k == 0), stop=(k == 7)))
            return [ins[-1]]
        p.step("tensor", lmm)

        def mmm(e, c0=c0):
            ins = []
            for k in range(8):
                ins.append(e.matmul(psM[:, :], dsb[k][:, c0:c0 + 128],
                                    la_ch[:, k * 8:(k + 1) * 8],
                                    start=(k == 0), stop=(k == 7)))
            return [ins[-1]]
        p.step("tensor", mmm)

        def sfm(e):
            return [e.tensor_add(lg[:, :], psL[:, :], rbt[:, :])]
        p.step("vector", sfm)

        def sfm2(e):
            return [e.reduce_max(cols[:, 0:1], lg[:, :], axis=AX)]
        p.step("vector", sfm2)

        def sfm3(e):
            return [e.tensor_scalar_mul(cols[:, 0:1], cols[:, 0:1], -1.0)]
        p.step("vector", sfm3)

        def sex(e):
            return [e.activation(lg[:, :], lg[:, :], AF.Exp, bias=cols[:, 0:1])]
        p.step("scalar", sex)

        def gd1(e):
            return [e.reduce_sum(cols[:, 1:2], lg[:, :], axis=AX),
                    e.tensor_copy(mid[:, :], psM[:, :])]
        p.step("vector", gd1, n=2)

        def gd2(e):
            return [e.reciprocal(cols[:, 1:2], cols[:, 1:2])]
        p.step("vector", gd2)

        def gd3(e):
            return [e.tensor_scalar(gat_b[:, :], lg[:, :], cols[:, 1:2], None, OP.mult)]
        p.step("vector", gd3)

        def gd4(e):
            return [e.tensor_copy(gat_f[:, :], gat_b[:, :])]
        p.step("vector", gd4)

        def mprod(e):
            ins = []
            for ex in range(8):
                ins.append(e.tensor_scalar(mp[:, ex * 8:(ex + 1) * 8], mid[:, :],
                                           gat_f[:, ex:ex + 1], None, OP.mult))
            return ins
        p.step("vector", mprod, n=8)

        def mcb(e):
            return [e.tensor_copy(mpb[:, :], mp[:, :])]
        p.step("vector", mcb)

        def mtr(e):
            return [e.transpose(psT[0:64, :], mpb[:, :], eye[:, :])]
        p.step("tensor", mtr)

        def mtc(e, c0=c0):
            return [e.tensor_copy(mpt[:, c0:c0 + 128], psT[0:64, :])]
        p.step("vector", mtc)

        if debug:
            if t == 0:
                dbg_g = p.outp("dbg_g", [TC, 8], BF16)
                dbg_m = p.outp("dbg_m", [TC, 8])
                dbg_mp = p.outp("dbg_mp", [TC, 64])

            def dst(e, c0=c0):
                return [e.dma_start(dbg_g[c0:c0 + 128, :], gat_b[:, :]),
                        e.dma_start(dbg_m[c0:c0 + 128, :], mid[:, :]),
                        e.dma_start(dbg_mp[c0:c0 + 128, :], mp[:, :])]
            p.step("sync", dst, n=3, dma=True)

    for m in range(8):
        for n in range(TC // 512):
            def omm(e, m=m, n=n):
                return [e.matmul(psO[:, :], lbb[:, m * 128:(m + 1) * 128],
                                 mpt[:, n * 512:(n + 1) * 512],
                                 start=True, stop=True)]
            p.step("tensor", omm)

            def ldr(e, m=m, n=n):
                return [e.dma_start(rtile[:, :],
                                    res_d[m * 128:(m + 1) * 128, n * 512:(n + 1) * 512])]
            p.step("sync", ldr, dma=True)

            def comb(e):
                return [e.scalar_tensor_tensor(otile[:, :], psO[:, :], 2.0,
                                               rtile[:, :], OP.mult, OP.add)]
            p.step("vector", comb)

            def st(e, m=m, n=n):
                return [e.dma_start(o_d[m * 128:(m + 1) * 128, n * 512:(n + 1) * 512],
                                    otile[:, :])]
            p.step("sync", st, dma=True)
    return p



def shards_tm(arr2d):
    return [np.ascontiguousarray(arr2d[c * TC:(c + 1) * TC]) for c in range(NCORES)]


def shards_fm(arr2d):
    return [np.ascontiguousarray(arr2d[:, c * TC:(c + 1) * TC]) for c in range(NCORES)]


def gather_tm(res, key):
    return np.concatenate([r[key] for r in res], axis=0)


def gather_fm(res, key):
    return np.concatenate([r[key] for r in res], axis=1)


def T_(a):
    return np.ascontiguousarray(a.T)


def host_shift(u2d):
    out = np.empty_like(u2d)
    out[:, :512] = u2d[:, :512]
    for b in range(2):
        s = b * 8192
        out[s, 512:] = 0
        out[s + 1:s + 8192, 512:] = u2d[s:s + 8192 - 1, 512:]
    return out


def _ln_launch_run(name, xin2d, res2d=None, g=None, b=None, double_ln=False,
                   out_f32=True, out_bf=False, out1_f32=False):
    p = build_ln_launch(name, TC, DM, res2d is not None, g is not None,
                        double_ln, out_f32, out_bf, out1_f32)
    xs = shards_tm(xin2d)
    rs = shards_tm(res2d) if res2d is not None else None
    maps = []
    for c in range(NCORES):
        m = {"xin": xs[c]}
        if rs is not None:
            m["res"] = rs[c]
        if g is not None:
            m["gt"] = np.ascontiguousarray(np.tile(f32c(g)[None, :], (128, 1)))
            m["bt"] = np.ascontiguousarray(np.tile(f32c(b)[None, :], (128, 1)))
        maps.append(m)
    return p.run(maps)


def _mm_launch_run(name, A, Bshards, bias=None, act=None, Rshards=None,
                   out_f32=False, out_bf=False, b_is_f32=False):
    K, M = A.shape
    N = Bshards[0].shape[1]
    p = build_mm_launch(name, K, M, N, bias is not None, act,
                        Rshards is not None, out_f32, out_bf, b_is_f32)
    one1 = np.ones((1, 1), dtype=np.float32)
    maps = []
    for c in range(NCORES):
        m = {"A": A, "B": Bshards[c], "one1": one1}
        if bias is not None:
            m["bias"] = np.ascontiguousarray(f32c(bias)[None, :])
        if Rshards is not None:
            m["R"] = Rshards[c]
        maps.append(m)
    return p.run(maps)



def kernel(**inp):
    _RUN_COUNT[0] = 0
    x = f32c(inp["L0_attn_representations"]).reshape(NTOK, DM)

    r = _ln_launch_run("ln1", x, out_f32=True, out_bf=True)
    a_f = gather_tm(r, "o_f32")
    a_b = gather_tm(r, "o_bf")
    aT_b = T_(a_b)
    aT_f = T_(a_f)

    WcT = T_(f32c(inp["L0_attn_Wc"]))
    r = _mm_launch_run("cp", WcT, shards_fm(aT_b), bias=inp["L0_attn_bc"],
                       act="lrelu", out_bf=True)
    cpT = gather_fm(r, "o_bf")

    pairs = [(b, h) for b in range(2) for h in range(16)]
    Qd = np.stack([cpT[h * 64:(h + 1) * 64, b * 8192:(b + 1) * 8192]
                   for (b, h) in pairs])
    Kd = np.stack([cpT[1024 + h * 64:1024 + (h + 1) * 64, b * 8192:(b + 1) * 8192]
                   for (b, h) in pairs])
    Vt = np.stack([T_(cpT[2048 + h * 512:2048 + (h + 1) * 512, b * 8192:(b + 1) * 8192])
                   for (b, h) in pairs])
    p = build_attn_launch("att1", 4, 8192, 512, causal=False, q_prescale=0.125)
    consts = dict(ones64=np.ones((64, 1), np.float32),
                  ones1r=np.ones((1, 128), np.float32),
                  one1=np.ones((1, 1), np.float32),
                  ones128b=np.ones((128, 1), BFNP))
    maps = []
    for c in range(NCORES):
        sl = slice(c * 4, c * 4 + 4)
        m = dict(Qd=np.ascontiguousarray(Qd[sl]), Kd=np.ascontiguousarray(Kd[sl]),
                 Vt=np.ascontiguousarray(Vt[sl]), **consts)
        maps.append(m)
    res = p.run(maps)
    O1 = np.concatenate([r["O"] for r in res], axis=0)

    OT = np.empty((8192, NTOK), dtype=BFNP)
    for i, (b, h) in enumerate(pairs):
        OT[h * 512:(h + 1) * 512, b * 8192:(b + 1) * 8192] = O1[i].T

    WoT = T_(f32c(inp["L0_attn_Wo"]))
    r = _mm_launch_run("proj1", WoT, shards_fm(OT), bias=inp["L0_attn_bo"],
                       act="lrelu", Rshards=shards_fm(aT_f), out_f32=True)
    tT = gather_fm(r, "o_f32")
    t_tm = T_(tT)

    r = _ln_launch_run("sand1", t_tm, res2d=x, double_ln=True, out_f32=True,
                       out_bf=True, out1_f32=True)
    y1 = gather_tm(r, "o1_f32")
    _DBG["y1"] = y1
    u_f = gather_tm(r, "o_f32")
    u_b = gather_tm(r, "o_bf")

    wqkv = f32c(inp["L0_ffn_wqkv"])
    wo2 = f32c(inp["L0_ffn_wo"])
    w1 = f32c(inp["L0_ffn_w1"])
    w2 = f32c(inp["L0_ffn_w2"])
    b1 = f32c(inp["L0_ffn_b1"])
    b2 = f32c(inp["L0_ffn_b2"])
    lng = f32c(inp["L0_ffn_ln_g"])
    lnb = f32c(inp["L0_ffn_ln_b"])
    qsc = f32c(inp["L0_ffn_q_scale"])
    ksc = f32c(inp["L0_ffn_k_scale"])

    DH, WIN = 64, 512
    hi = np.arange(0, DH, 2, dtype=np.float32)
    invf = (1.0 / 10000.0 ** (hi / DH)).astype(np.float32)
    tpos = np.arange(8192, dtype=np.float32)
    fr = tpos[:, None] * invf[None, :]
    fr = np.concatenate([fr, fr], axis=1)
    base = ((hi + 0.4 * DH) / (1.4 * DH)).astype(np.float32)
    power = ((tpos - 8192 // 2) / float(WIN // 2)).astype(np.float32)
    scale = (base[None, :] ** power[:, None]).astype(np.float32)
    scale = np.concatenate([scale, scale], axis=1)
    cosf, sinf = np.cos(fr).astype(np.float32), np.sin(fr).astype(np.float32)
    qct = np.tile(cosf * scale, (2, 8))
    qst = np.tile(sinf * scale, (2, 8))
    kct = np.tile(cosf / scale, (2, 8))
    kst = np.tile(sinf / scale, (2, 8))

    ii = np.arange(512)[:, None]
    jj = np.arange(1024)[None, :]
    prev = jj < 512
    keep_mid = np.where(prev, jj >= ii, (jj - 512) <= ii)
    keep_first = np.where(prev, False, (jj - 512) <= ii)
    M_mid = np.where(keep_mid, 0.0, -1e30).astype(np.float32)
    M_first = np.where(keep_first, 0.0, -1e30).astype(np.float32)

    eye_b = np.eye(128, dtype=BFNP)

    ucur_f, ucur_b = u_f, u_b
    for l in range(2):
        h_b = host_shift(ucur_b)
        hT_b = T_(h_b)
        r = _mm_launch_run(f"qkv2_{l}", wqkv[l], shards_fm(hT_b), out_bf=True)
        qkvT = gather_fm(r, "o_bf")
        q_tm = T_(qkvT[0:512])
        k_tm = T_(qkvT[512:1024])
        v_tm = T_(qkvT[1024:1536])

        p = build_rope2_launch(f"rope2_{l}")
        qs_t = np.ascontiguousarray(np.tile(np.tile(qsc[l], 8)[None, :], (128, 1)))
        ks_t = np.ascontiguousarray(np.tile(np.tile(ksc[l], 8)[None, :], (128, 1)))
        maps = []
        for c in range(NCORES):
            sl = slice(c * TC, (c + 1) * TC)
            maps.append(dict(q=np.ascontiguousarray(q_tm[sl]),
                             k=np.ascontiguousarray(k_tm[sl]),
                             qc=np.ascontiguousarray(qct[sl]),
                             qs=np.ascontiguousarray(qst[sl]),
                             kc=np.ascontiguousarray(kct[sl]),
                             ks=np.ascontiguousarray(kst[sl]),
                             qscale=qs_t, kscale=ks_t))
        res = p.run(maps)
        qp = gather_tm(res, "qo")
        kp = gather_tm(res, "ko")

        p = build_latt_launch(f"latt_{l}")
        maps = []
        for c in range(NCORES):
            s0 = c * TC
            qdc = T_(qp[s0:s0 + TC])
            if c % 4 == 0:
                kh = np.concatenate([np.zeros((512, 512), BFNP), kp[s0:s0 + TC]], axis=0)
                vh = np.concatenate([np.zeros((512, 512), BFNP), v_tm[s0:s0 + TC]], axis=0)
            else:
                kh = kp[s0 - 512:s0 + TC]
                vh = v_tm[s0 - 512:s0 + TC]
            kdc = T_(kh)
            masks = np.stack([M_first if (c % 4 == 0 and w == 0) else M_mid
                              for w in range(4)])
            maps.append(dict(Qd=qdc, Kd=kdc, Vt=np.ascontiguousarray(vh),
                             maskadd=masks, eyeb=eye_b))
        res = p.run(maps)
        oT = gather_fm(res, "oT")

        r = _mm_launch_run(f"proj2_{l}", wo2[l], shards_fm(oT),
                           Rshards=shards_fm(T_(ucur_f)), out_f32=True,
                           out_bf=True)
        umidT = gather_fm(r, "o_f32")
        umid = T_(umidT)
        umid_b = T_(gather_fm(r, "o_bf"))

        h2f = host_shift(umid)
        r = _ln_launch_run(f"lngb_{l}", h2f, g=lng[l], b=lnb[l],
                           out_f32=False, out_bf=True)
        h2n_b = gather_tm(r, "o_bf")

        r = _mm_launch_run(f"ff1_{l}", w1[l], shards_fm(T_(h2n_b)),
                           bias=b1[l], out_f32=True)
        h3T = gather_fm(r, "o_f32")

        p = build_geglu_launch(f"geglu_{l}")
        maps = []
        for c in range(NCORES):
            sl = slice(c * TC, (c + 1) * TC)
            maps.append(dict(a=np.ascontiguousarray(h3T[0:2730, sl]),
                             g=np.ascontiguousarray(h3T[2730:5460, sl])))
        res = p.run(maps)
        h4T = gather_fm(res, "o_bf")

        r = _mm_launch_run(f"ff2_{l}", w2[l], shards_fm(h4T), bias=b2[l],
                           Rshards=shards_fm(umidT), out_f32=True, out_bf=True)
        ucur_f = T_(gather_fm(r, "o_f32"))
        ucur_b = T_(gather_fm(r, "o_bf"))
        _DBG[f"u_{l}"] = ucur_f
        _DBG[f"qkv_{l}"] = (q_tm, k_tm, v_tm)
        _DBG[f"rope_{l}"] = (qp, kp)
        _DBG[f"oT_{l}"] = oT
        _DBG[f"umid_{l}"] = umid
        _DBG[f"h2n_{l}"] = h2n_b
        _DBG[f"h4_{l}"] = h4T

    r = _ln_launch_run("sand2", ucur_f, res2d=y1, double_ln=True,
                       out_f32=False, out_bf=True, out1_f32=True)
    x2 = gather_tm(r, "o1_f32")
    _DBG["x2"] = x2
    c0_b = gather_tm(r, "o_bf")

    DK = 64
    half = DK // 2
    invf3 = (10000.0 ** (-np.arange(half, dtype=np.float32) * 2.0 / DK)).astype(np.float32)
    pos = np.arange(8192, dtype=np.float32)
    ang = pos[:, None] * invf3[None, :]
    cos3 = np.cos(ang).astype(np.float32)
    sin3 = np.sin(ang).astype(np.float32)
    qct3 = np.tile(cos3 * 0.125, (2, 16))
    qst3 = np.tile(sin3 * 0.125, (2, 16))
    kct3 = np.tile(cos3, (2, 16))
    kst3 = np.tile(sin3, (2, 16))

    perm = np.concatenate([np.concatenate([h * 64 + np.arange(0, 64, 2),
                                           h * 64 + np.arange(1, 64, 2)])
                           for h in range(16)])
    WqT = T_(f32c(inp["L1_attn_Wq"]))[:, perm]
    WkT = T_(f32c(inp["L1_attn_Wk"]))[:, perm]
    WvT = T_(f32c(inp["L1_attn_Wv"]))
    A3 = np.ascontiguousarray(np.concatenate([WqT, WkT, WvT], axis=1))
    r = _mm_launch_run("qkv3", A3, shards_fm(T_(c0_b)), out_bf=True)
    qkv3T = gather_fm(r, "o_bf")
    q3 = T_(qkv3T[0:1024])
    k3 = T_(qkv3T[1024:2048])
    v3T = qkv3T[2048:3072]

    p = build_rope3_launch("rope3")
    maps = []
    for c in range(NCORES):
        sl = slice(c * TC, (c + 1) * TC)
        maps.append(dict(q=np.ascontiguousarray(q3[sl]),
                         k=np.ascontiguousarray(k3[sl]),
                         qc=np.ascontiguousarray(qct3[sl]),
                         qs=np.ascontiguousarray(qst3[sl]),
                         kc=np.ascontiguousarray(kct3[sl]),
                         ks=np.ascontiguousarray(kst3[sl])))
    res = p.run(maps)
    qp3T = T_(gather_tm(res, "qo"))
    kp3T = T_(gather_tm(res, "ko"))

    Qd3 = np.stack([qp3T[h * 64:(h + 1) * 64, b * 8192:(b + 1) * 8192]
                    for (b, h) in pairs])
    Kd3 = np.stack([kp3T[h * 64:(h + 1) * 64, b * 8192:(b + 1) * 8192]
                    for (b, h) in pairs])
    Vt3 = np.stack([T_(v3T[h * 64:(h + 1) * 64, b * 8192:(b + 1) * 8192])
                    for (b, h) in pairs])

    jjj = np.arange(128)[:, None]
    iii = np.arange(512)[None, :]
    md = np.stack([(jjj <= (iii - 128 * d)).astype(np.float32).astype(BFNP)
                   for d in range(4)])
    p = build_attn_launch("att3", 4, 8192, 64, causal=True)
    maps = []
    for c in range(NCORES):
        sl = slice(c * 4, c * 4 + 4)
        maps.append(dict(Qd=np.ascontiguousarray(Qd3[sl]),
                         Kd=np.ascontiguousarray(Kd3[sl]),
                         Vt=np.ascontiguousarray(Vt3[sl]),
                         Mdiag=md, **consts))
    res = p.run(maps)
    O3 = np.concatenate([r["O"] for r in res], axis=0)
    OT3 = np.empty((1024, NTOK), dtype=BFNP)
    for i, (b, h) in enumerate(pairs):
        OT3[h * 64:(h + 1) * 64, b * 8192:(b + 1) * 8192] = O3[i].T

    r = _mm_launch_run("proj3", T_(f32c(inp["L1_attn_Wo"])), shards_fm(OT3),
                       out_f32=True)
    att3 = T_(gather_fm(r, "o_f32"))

    r = _ln_launch_run("sand3", att3, res2d=x2, double_ln=True,
                       out_f32=False, out_bf=True, out1_f32=True)
    x3 = gather_tm(r, "o1_f32")
    _DBG["x3"] = x3
    d0_b = gather_tm(r, "o_bf")
    d0T_b = T_(d0_b)

    r = _mm_launch_run("base4", T_(f32c(inp["L1_ffn_base_w"])),
                       shards_fm(d0T_b), bias=inp["L1_ffn_base_b"], out_f32=True)
    resT = gather_fm(r, "o_f32")

    p = build_lora_launch("lora4")
    rwT = T_(f32c(inp["L1_ffn_router_w"]))
    laA = f32c(inp["L1_ffn_lora_A"])
    lb2 = f32c(inp["L1_ffn_lora_B"]).reshape(64, 1024)
    rb_t = np.ascontiguousarray(np.tile(f32c(inp["L1_ffn_router_b"])[None, :], (128, 1)))
    maps = []
    for c in range(NCORES):
        sl = slice(c * TC, (c + 1) * TC)
        maps.append(dict(dT=np.ascontiguousarray(d0T_b[:, sl]),
                         resT=np.ascontiguousarray(resT[:, sl]),
                         rwT=rwT, rb=rb_t, loraA=laA, loraB2d=lb2, eyeb=eye_b))
    res = p.run(maps)
    out4 = T_(gather_fm(res, "o_f32"))

    r = _ln_launch_run("sand4", out4, res2d=x3, double_ln=True,
                       out_f32=False, out_bf=False, out1_f32=True)
    final = gather_tm(r, "o1_f32")
    return np.ascontiguousarray(final, dtype=np.float32).reshape(2, 8192, 1024)
